# version 28 (final)
# speedup vs baseline: 1.0611x; 1.0028x over previous
import contextlib
import numpy as np
import ml_dtypes
import concourse.bass as bass
import concourse.mybir as mybir
from concourse.bass_utils import run_bass_kernel_spmd

F32 = mybir.dt.float32
BF16 = mybir.dt.bfloat16
AF = mybir.ActivationFunctionType
OP = mybir.AluOpType
AX = mybir.AxisListType

S = 2048
D = 1024
NT = S // 128
NC = D // 128
NH = 16
DH = 64
NP = NH // 2
RK = 256
DF = 4096
TC = 512
NTC = S // TC
NEG = -1.0e38
EPS = 1e-5

PHASES = 3


def build_nc(debug=()):
    nc = bass.Bass("TRN2", target_bir_lowering=False, dynamic_dma_scratch_size=2048)

    din = {}
    for name, shape in [("xT", [D, S]), ("qw", [D, D]), ("kw", [D, D]), ("vw", [D, D]),
                        ("ow", [D, D]), ("qb", [1, D]), ("kb", [1, D]), ("vb", [1, D]),
                        ("ob", [1, D]), ("u1", [D, RK]), ("v1", [RK, DF]), ("b1", [1, DF]),
                        ("u2", [DF, RK]), ("v2", [RK, D]), ("b2", [1, D]),
                        ("ident", [128, 128]), ("masks", [2, 128, 128])]:
        din[name] = nc.dram_tensor(name, shape, F32, kind="ExternalInput")
    din["masksb"] = nc.dram_tensor("masksb", [2, 128, 128], BF16, kind="ExternalInput")
    d_out = nc.dram_tensor("outT", [D, S], F32, kind="ExternalOutput")
    v_scr = nc.dram_tensor("vscr", [16, 128, 1024], BF16, kind="Internal")

    dbg = {}
    for name in debug:
        shape = {"x1": [D, S], "x2": [D, S]}[name]
        dbg[name] = nc.dram_tensor("dbg_" + name, shape, F32, kind="ExternalOutput")

    gctx = contextlib.ExitStack()
    uid = [0]

    import os
    _dbg_alloc = os.environ.get("KDBG_ALLOC")
    _tot = [0]

    def mk(stack, name, shape, dt=F32, psum=False):
        uid[0] += 1
        f = nc.psum_tensor if psum else nc.sbuf_tensor
        if _dbg_alloc and not psum:
            nb = int(np.prod(shape[1:])) * (4 if dt == F32 else 2)
            _tot[0] += nb

            @contextlib.contextmanager
            def _wrap(nb=nb, nm=name):
                with f(f"{nm}_{uid[0]}", shape, dt) as t:
                    print(f"+ {nm:12s} {nb:7d} tot={_tot[0]}")
                    try:
                        yield t
                    finally:
                        _tot[0] -= nb
                        print(f"- {nm:12s} tot={_tot[0]}")
            return stack.enter_context(_wrap())
        return stack.enter_context(f(f"{name}_{uid[0]}", shape, dt))

    with gctx:
        x = mk(gctx, "x", [128, NC, S])
        r_ones = mk(gctx, "r_ones", [1, TC], BF16)
        ones_bf = mk(gctx, "ones_bf", [128, 1], BF16)
        ones_f32 = mk(gctx, "ones_f32", [1, 128])
        neg_bf = mk(gctx, "neg_bf", [128, 1], BF16)
        eps_col = mk(gctx, "eps_col", [128, 1])
        epsD_col = mk(gctx, "epsD_col", [128, 1])
        halflnD_col = mk(gctx, "halflnD_col", [128, 1])
        ident = mk(gctx, "ident", [128, 128])
        identb = mk(gctx, "identb", [128, 128], BF16)

        p_big = mk(gctx, "p_big", [128, 2048], psum=True)
        p_big2 = mk(gctx, "p_big2", [128, 2048], psum=True)

        sems = {}
        for name in ["ld", "cast", "pe", "ev", "act", "row", "st", "dmx", "wc", "wp"] + [
                f"l{i}" for i in range(10)]:
            h = gctx.enter_context(nc.semaphore("s_" + name))
            sems[name] = [h, 0]

        class _H:
            def __init__(self, pair): self.p = pair
            @property
            def h(self): return self.p[0]
            @property
            def v(self): return self.p[1]
            def inc(self, inst, n=1):
                inst.then_inc(self.p[0], n)
                self.p[1] += n
                return self.p[1]
        LD, CAST, PE, EV, ACT, ROW, ST, DMX, WC, WP = (_H(sems[k]) for k in
            ["ld", "cast", "pe", "ev", "act", "row", "st", "dmx", "wc", "wp"])
        LSL = [_H(sems[f"l{i}"]) for i in range(10)]

        with nc.Block() as blk:
            @blk.sync
            def _(sync):
                LD.inc(sync.dma_start(ident[:, :], din["ident"].ap()), 16)
                sync.wait_ge(LD.h, LD.v)

            @blk.gpsimd
            def _(gp):
                gp.memset(ones_bf[:, :], 1.0)
                gp.memset(neg_bf[:, :], -1.0)
                gp.memset(ones_f32[:, :], 1.0)
                gp.memset(r_ones[:, :], 1.0)
                gp.memset(eps_col[:, :], EPS)
                gp.memset(epsD_col[:, :], float(D * EPS))
                gp.memset(halflnD_col[:, :], float(0.5 * np.log(D)))

        with nc.Block() as blk:
            @blk.vector
            def _(vector):
                vector.tensor_copy(identb[:, :], ident[:, :])

        def emit_ln_stats(r_rstd, r_mr, extra_loads=None, stage=None,
                          x_load=False, post_pe=None, post_dve=None, wsum2=None):
            extra_loads = extra_loads or []
            with contextlib.ExitStack() as lctx:
                xb_s = mk(lctx, "xb_s", [128, 2, 1024], BF16)
                sq_s = mk(lctx, "sq_s", [128, 2, TC], BF16)
                r_tmpa = mk(lctx, "r_tmpa", [1, S])
                r_tmpb = r_rstd

                nslot = stage.shape[-1] // 512 if stage is not None else 0
                na = (nslot * 3 + 4) // 8
                nd = (nslot * 3 + 4) // 8
                npl = nslot - na - nd
                slots = {"A": list(range(0, na)), "D": list(range(na, na + nd)),
                         "P": list(range(na + nd, nslot))}
                cyc = ["A", "P", "A", "D"] if npl else ["A", "D"]
                assign = []
                cnt = {"A": 0, "D": 0, "P": 0}
                for i in range(len(extra_loads)):
                    st = cyc[i % len(cyc)]
                    assign.append((st, cnt[st]))
                    cnt[st] += 1
                views = []
                for i, (src_ap, dst_ap, sc) in enumerate(extra_loads):
                    fs = list(src_ap.shape[1:])
                    tot = int(np.prod(fs))
                    assert tot <= 512, (tot,)
                    st, k = assign[i]
                    s = slots[st][k % len(slots[st])]
                    sv = stage[0:src_ap.shape[0], 512 * s:512 * s + tot]
                    if len(fs) == 2:
                        sv = sv.rearrange("p (a b) -> p a b", a=fs[0])
                    views.append(sv)
                nw = len(extra_loads)

                with nc.Block() as blk:
                    pe0, cast0, row0s = PE.v, CAST.v, ROW.v
                    ch = {"A": ACT, "D": WC, "P": WP}
                    ch0 = {st: ch[st].v for st in ch}
                    l0 = [LSL[s].v for s in range(nslot)]

                    def pwait(eng, i):
                        st, k = assign[i]
                        eng.wait_ge(ch[st].h, ch0[st] + k + 1)

                    def cast_piece(eng, i):
                        st, k = assign[i]
                        s = slots[st][k % len(slots[st])]
                        eng.wait_ge(LSL[s].h, l0[s] + 16 * (k // len(slots[st]) + 1))
                        sa, da, sc = extra_loads[i]
                        if st == "A":
                            ch[st].inc(eng.activation(da, views[i], AF.Copy,
                                                      scale=1.0 if sc is None else sc))
                        elif sc is None:
                            ch[st].inc(eng.tensor_copy(da, views[i]))
                        else:
                            ch[st].inc(eng.tensor_scalar(da, views[i], sc, None, OP.mult))

                    xch = [LD, DMX, LSL[8], LSL[9]]
                    x0s = [c.v for c in xch]
                    cast_x0 = CAST.v
                    xgate = {10: 4, 14: 5, 18: 6, 22: 7}

                    @blk.sync
                    def _(sync):
                        def xdma(c):
                            xch[c % 4].inc(sync.dma_start(
                                x[:, c, :], din["xT"].ap()[128 * c:128 * (c + 1), :]), 16)
                        if x_load:
                            for c in range(4):
                                xdma(c)
                        for i in range(nw):
                            st, k = assign[i]
                            ns = len(slots[st])
                            if k >= ns:
                                sync.wait_ge(ch[st].h, ch0[st] + k - ns + 1)
                            LSL[slots[st][k % ns]].inc(sync.dma_start(views[i], extra_loads[i][0]), 16)
                            if x_load and i in xgate:
                                c = xgate[i]
                                sync.wait_ge(CAST.h, cast_x0 + 2 * (c - 3))
                                xdma(c)


                    @blk.scalar
                    def _(scalar):
                        for i in range(nw):
                            if assign[i][0] == "A":
                                cast_piece(scalar, i)

                    @blk.tensor
                    def _(tensor):
                        for c in range(NC):
                            for j in range(4):
                                t = 4 * c + j
                                hs = 2 * c + j // 2
                                sl = slice(512 * j, 512 * (j + 1))
                                ssl = slice(512 * (j % 2), 512 * (j % 2 + 1))
                                tensor.wait_ge(CAST.h, cast0 + hs + 1)
                                PE.inc(tensor.matmul(p_big[0:1, sl], ones_bf[:, :],
                                                     xb_s[:, hs % 2, ssl],
                                                     start=(c == 0), stop=(c == NC - 1)))
                                tensor.wait_ge(ROW.h, row0s + t + 1)
                                PE.inc(tensor.matmul(p_big2[0:1, sl], ones_bf[:, :], sq_s[:, t % 2, :],
                                                     start=(c == 0), stop=(c == NC - 1)))
                        for fn in (post_pe or []):
                            fn(tensor, pwait)

                    @blk.vector
                    def _(vector):
                        dpieces = [i for i in range(nw) if assign[i][0] == "D"]
                        di = 0
                        for c in range(NC):
                            for half in range(2):
                                hs = 2 * c + half
                                if hs >= 2:
                                    vector.wait_ge(PE.h, pe0 + 4 * (hs - 1))
                                if x_load and half == 0:
                                    vector.wait_ge(xch[c % 4].h,
                                                   x0s[c % 4] + 16 * (c // 4 + 1))
                                cc = CAST.inc(vector.tensor_copy(
                                    xb_s[:, hs % 2, :], x[:, c, 1024 * half:1024 * (half + 1)]))
                                vector.wait_ge(CAST.h, cc)
                                for jj in range(2):
                                    t = 4 * c + 2 * half + jj
                                    sl = slice(512 * jj, 512 * (jj + 1))
                                    if t >= 2:
                                        vector.wait_ge(PE.h, pe0 + 2 * t - 2)
                                    ROW.inc(vector.tensor_tensor(
                                        sq_s[:, t % 2, :], xb_s[:, hs % 2, sl],
                                        xb_s[:, hs % 2, sl], OP.mult))
                                if hs >= 3 and di < len(dpieces):
                                    cast_piece(vector, dpieces[di])
                                    di += 1
                        for i in dpieces[di:]:
                            cast_piece(vector, i)
                        for fn in (post_dve or []):
                            fn(vector)

                    @blk.gpsimd
                    def _(gp):
                        for i in range(nw):
                            if assign[i][0] == "P":
                                cast_piece(gp, i)


                with nc.Block() as blk:
                    act0, row0 = ACT.v, ROW.v
                    ws_mark = PE.v + 1

                    @blk.vector
                    def _(vector):
                        r1 = ROW.inc(vector.tensor_scalar(r_tmpa[0:1, :], p_big[0:1, 0:S], 1.0 / D, None, OP.mult))
                        vector.wait_ge(ROW.h, r1)
                        r3 = ROW.inc(vector.tensor_tensor(r_tmpb[0:1, :], r_tmpa[0:1, :], p_big[0:1, 0:S],
                                                          OP.mult))
                        vector.wait_ge(ROW.h, r3)
                        ROW.inc(vector.tensor_tensor(r_tmpb[0:1, :], p_big2[0:1, 0:S], r_tmpb[0:1, :], OP.subtract))
                        if wsum2 is not None:
                            vector.wait_ge(PE.h, ws_mark)
                            ROW.inc(vector.tensor_copy(wsum2[0], p_big[0:1, 0:wsum2[2]]))

                    if wsum2 is not None:
                        @blk.tensor
                        def _(tensor):
                            tensor.wait_ge(ROW.h, row0 + 3)
                            mm = None
                            for c in range(NC):
                                mm = tensor.matmul(p_big[0:1, 0:wsum2[2]], neg_bf[:, :],
                                                   wsum2[1](c), start=(c == 0),
                                                   stop=(c == NC - 1),
                                                   skip_group_check=True)
                            PE.inc(mm)

                    @blk.scalar
                    def _(scalar):
                        scalar.wait_ge(ROW.h, row0 + 3)
                        a2 = ACT.inc(scalar.activation(r_tmpb[0:1, :], r_tmpb[0:1, :], AF.Ln, bias=epsD_col[0:1, :]))
                        scalar.wait_ge(ACT.h, a2)
                        ACT.inc(scalar.activation(r_rstd[0:1, :], r_tmpb[0:1, :], AF.Exp, scale=-0.5,
                                                  bias=halflnD_col[0:1, :]))

                with nc.Block() as blk:
                    @blk.vector
                    def _(vector):
                        ROW.inc(vector.tensor_tensor(r_mr[0:1, :], r_tmpa[0:1, :], r_rstd[0:1, :], OP.mult))

        def emit_wsum_row(dst_row, w_chunk, nchunks, width):
            with nc.Block() as blk:
                @blk.tensor
                def _(tensor):
                    for c in range(nchunks):
                        for j in range(0, width, 512):
                            jw = min(512, width - j)
                            PE.inc(tensor.matmul(p_big[0:1, j:j + jw], neg_bf[:, :],
                                                 w_chunk(c)[:, j:j + jw],
                                                 start=(c == 0), stop=(c == nchunks - 1)))

                @blk.vector
                def _(vector):
                    vector.wait_ge(PE.h, PE.v)
                    vector.tensor_copy(dst_row, p_big[0:1, 0:width])

        def emit_yprime(dst, r_rstd, tci, chunk_local=False):
            sl = slice(TC * tci, TC * (tci + 1))
            dsl = slice(0, TC) if chunk_local else sl
            with nc.Block() as blk:
                pe0 = PE.v

                @blk.tensor
                def _(tensor):
                    PE.inc(tensor.matmul(p_big2[:, 0:TC], ones_f32[0:1, :], r_rstd[0:1, sl],
                                         start=True, stop=True))

                @blk.vector
                def _(vector):
                    vector.wait_ge(PE.h, pe0 + 1)
                    for c in range(NC):
                        CAST.inc(vector.tensor_tensor(dst[:, c, dsl], x[:, c, sl], p_big2[:, 0:TC], OP.mult))

        def load_stage(stage, pairs, scale=None):
            views = []
            half = stage.shape[-1] // 2
            for i, (src_ap, dst_ap) in enumerate(pairs):
                fs = list(src_ap.shape[1:])
                tot = int(np.prod(fs))
                npart = src_ap.shape[0]
                if tot <= half and len(pairs) > 1:
                    sv = stage[0:npart, half * (i % 2):half * (i % 2) + tot]
                else:
                    sv = stage[0:npart, 0:tot]
                if len(fs) == 2:
                    sv = sv.rearrange("p (a b) -> p a b", a=fs[0])
                views.append(sv)
            dbl = all(int(np.prod(p[0].shape[1:])) <= half for p in pairs) and len(pairs) > 1
            with nc.Block() as blk:
                ld0, cast0 = LD.v, CAST.v
                n = len(pairs)

                @blk.sync
                def _(sync):
                    if not dbl:
                        for i in range(n):
                            if i >= 1:
                                sync.wait_ge(CAST.h, cast0 + i)
                            LD.inc(sync.dma_start(views[i], pairs[i][0]), 16)
                    else:
                        for g in range((n + 1) // 2):
                            if g >= 1:
                                sync.wait_ge(CAST.h, cast0 + 2 * g)
                            for i in (2 * g, 2 * g + 1):
                                if i < n:
                                    LD.inc(sync.dma_start(views[i], pairs[i][0]), 16)

                @blk.vector
                def _(vector):
                    def cast_one(i):
                        if scale is None:
                            CAST.inc(vector.tensor_copy(pairs[i][1], views[i]))
                        else:
                            CAST.inc(vector.tensor_scalar(pairs[i][1], views[i], scale, None, OP.mult))
                    if not dbl:
                        for i in range(n):
                            vector.wait_ge(LD.h, ld0 + 16 * (i + 1))
                            cast_one(i)
                    else:
                        for g in range((n + 1) // 2):
                            hi = min(n, 2 * g + 2)
                            vector.wait_ge(LD.h, ld0 + 16 * hi)
                            for i in range(2 * g, hi):
                                cast_one(i)

        def emit_ffn_sublayer(tag):
            with contextlib.ExitStack() as fctx:
                u1b = mk(fctx, "u1b", [128, NC, RK], BF16)
                v1b = mk(fctx, "v1b", [128, 2, DF], BF16)
                u2b = mk(fctx, "u2b", [128, 32, RK], BF16)
                v2b = mk(fctx, "v2b", [128, 2, D], BF16)
                wstage = mk(fctx, "wstage", [128, 4096])
                ybc = mk(fctx, "ybc", [128, 2, NC, TC], BF16)
                pb = mk(fctx, "pb", [128, 2, 2, TC], BF16)
                hb = mk(fctx, "hb", [128, 4, TC], BF16)
                qmb = mk(fctx, "qmb", [128, 2, 2, TC], BF16)
                r_b1 = mk(fctx, "r_b1", [1, DF], BF16)
                r_b2h = mk(fctx, "r_b2h", [1, D], BF16)
                r_wsu1 = mk(fctx, "r_wsu1", [1, RK], BF16)
                r_rstd = mk(fctx, "r_rstd", [1, S])
                r_mr = mk(fctx, "r_mr", [1, S], BF16)
                bcs = mk(fctx, "bcs", [128, 2, TC])

                ffw = din["u1"].ap().rearrange("(c p) r -> p c r", p=128)
                fv1 = din["v1"].ap().rearrange("(c p) f -> p c f", p=128)
                fu2 = din["u2"].ap().rearrange("(c p) r -> p c r", p=128)
                fv2 = din["v2"].ap().rearrange("(c p) m -> p c m", p=128)
                wl = ([(ffw[:, 2 * i:2 * i + 2, :], u1b[:, 2 * i:2 * i + 2, :], None)
                       for i in range(4)]
                      + [(fv1[:, c:c + 1, 512 * i:512 * (i + 1)],
                          v1b[:, c:c + 1, 512 * i:512 * (i + 1)], None)
                         for i in range(8) for c in range(2)]
                      + [(fu2[:, 2 * i:2 * i + 2, :], u2b[:, 2 * i:2 * i + 2, :], None)
                         for i in range(16)]
                      + [(din["b1"].ap()[0:1, 512 * i:512 * (i + 1)],
                          r_b1[0:1, 512 * i:512 * (i + 1)], None) for i in range(8)]
                      + [(fv2[:, c:c + 1, 512 * i:512 * (i + 1)],
                          v2b[:, c:c + 1, 512 * i:512 * (i + 1)], 0.5)
                         for c in range(2) for i in range(2)]
                      + [(din["b2"].ap()[0:1, 512 * i:512 * (i + 1)],
                          r_b2h[0:1, 512 * i:512 * (i + 1)], 0.5) for i in range(2)])
                emit_ln_stats(r_rstd, r_mr, extra_loads=wl, stage=wstage,
                              x_load=(tag == "f1"),
                              wsum2=(r_wsu1[0:1, 0:RK], lambda c: u1b[:, c, :], RK))

                plan = [("bc", 0), ("bc", 1)]
                for c in range(NTC):
                    plan += [("u1", c, 0), ("u1", c, 1)]
                    for fp in range(16):
                        plan += [("v1", c, 2 * fp), ("v1", c, 2 * fp + 1)]
                        if c >= 1 and fp < 4:
                            plan += [("v2", c - 1, 2 * fp), ("v2", c - 1, 2 * fp + 1)]
                        if fp >= 1:
                            plan += [("u2", c, 2 * fp - 2, 0), ("u2", c, 2 * fp - 2, 1),
                                     ("u2", c, 2 * fp - 1, 0), ("u2", c, 2 * fp - 1, 1)]
                    plan += [("u2", c, 30, 0), ("u2", c, 30, 1), ("u2", c, 31, 0), ("u2", c, 31, 1)]
                    if c + 2 < NTC:
                        plan.append(("bc", c + 2))
                plan += [("v2", NTC - 1, dm) for dm in range(NC)]

                dplan = [("bcev", 0), ("bcev", 1)]
                for c in range(NTC):
                    dplan.append(("P", c))
                    if c >= 1:
                        dplan += [("xadd", c - 1, dp) for dp in range(4)]
                    dplan.append(("Qm", c))
                    if c + 2 < NTC:
                        dplan.append(("bcev", c + 2))
                dplan += [("xadd", NTC - 1, dp) for dp in range(4)]

                with nc.Block() as blk:
                    fp0, ws0, xa0, ca0 = PE.v, EV.v, ACT.v, CAST.v
                    pos = {ev: fp0 + i + 1 for i, ev in enumerate(plan)}
                    dpos = {ev: ws0 + i + 1 for i, ev in enumerate(dplan)}

                    if tag == "f2":
                        @blk.sync
                        def _(sync):
                            for c in range(NTC):
                                for dp in range(4):
                                    sync.wait_ge(EV.h, dpos[("xadd", c, dp)])
                                    for f in (2 * dp, 2 * dp + 1):
                                        ST.inc(sync.dma_start(
                                            d_out.ap()[128 * f:128 * (f + 1),
                                                       512 * c:512 * (c + 1)],
                                            x[:, f, 512 * c:512 * (c + 1)]), 16)

                    @blk.tensor
                    def _(tensor):
                        for ev in plan:
                            if ev[0] == "bc":
                                c = ev[1]
                                sl = slice(TC * c, TC * (c + 1))
                                if c == 1:
                                    tensor.wait_ge(EV.h, dpos[("bcev", 0)])
                                if c >= 2:
                                    tensor.wait_ge(EV.h, dpos[("Qm", c - 2)])
                                PE.inc(tensor.matmul(p_big2[:, 0:TC], ones_f32[0:1, :],
                                                     r_rstd[0:1, sl], start=True, stop=True))
                            elif ev[0] == "u1":
                                c, m = ev[1], ev[2]
                                sl = slice(TC * c, TC * (c + 1))
                                if m == 0:
                                    tensor.wait_ge(CAST.h, ca0 + 8 * (c + 1))
                                    if c >= 1:
                                        tensor.wait_ge(ACT.h, xa0 + 16 * c)
                                psl = p_big[:, 512 * m:512 * m + TC]
                                for cc in range(NC):
                                    tensor.matmul(psl, u1b[:, cc, 128 * m:128 * (m + 1)],
                                                  ybc[:, c % 2, cc, :], start=(cc == 0), stop=False)
                                PE.inc(tensor.matmul(psl, r_wsu1[0:1, 128 * m:128 * (m + 1)],
                                                     r_mr[0:1, sl], start=False, stop=True))
                            elif ev[0] == "v1":
                                c, fc = ev[1], ev[2]
                                fp, j = fc // 2, fc % 2
                                if fc == 0:
                                    tensor.wait_ge(EV.h, dpos[("P", c)])
                                if fp >= 2 and j == 0:
                                    tensor.wait_ge(ACT.h, xa0 + 16 * c + fp - 1)
                                psl = p_big[:, 1024 * (fp % 2) + 512 * j:1024 * (fp % 2) + 512 * j + TC]
                                for rc in range(2):
                                    tensor.matmul(psl, v1b[:, rc, 128 * fc:128 * (fc + 1)],
                                                  pb[:, c % 2, rc, :], start=(rc == 0), stop=False)
                                PE.inc(tensor.matmul(psl, r_b1[0:1, 128 * fc:128 * (fc + 1)],
                                                     r_ones[0:1, 0:TC], start=False, stop=True))
                            elif ev[0] == "u2":
                                c, fc, m = ev[1], ev[2], ev[3]
                                if m == 0:
                                    tensor.wait_ge(ACT.h, xa0 + 16 * c + fc // 2 + 1)
                                if fc == 0 and m == 0 and c + 1 < NTC:
                                    tensor.wait_ge(EV.h, dpos[("bcev", c + 1)])
                                PE.inc(tensor.matmul(p_big2[:, 512 * m:512 * m + TC],
                                                     u2b[:, fc, 128 * m:128 * (m + 1)],
                                                     hb[:, fc % 4, :],
                                                     start=(fc == 0), stop=(fc == 31)))
                            elif ev[0] == "v2":
                                c, dm = ev[1], ev[2]
                                if dm == 0:
                                    tensor.wait_ge(EV.h, dpos[("Qm", c)])
                                if dm >= 2:
                                    tensor.wait_ge(EV.h, dpos[("xadd", c, (dm - 2) // 2)])
                                psl = p_big2[:, 1024 + 512 * (dm % 2):1024 + 512 * (dm % 2) + TC]
                                for rc in range(2):
                                    tensor.matmul(psl, v2b[:, rc, 128 * dm:128 * (dm + 1)],
                                                  qmb[:, c % 2, rc, :], start=(rc == 0), stop=False)
                                PE.inc(tensor.matmul(psl, r_b2h[0:1, 128 * dm:128 * (dm + 1)],
                                                     r_ones[0:1, 0:TC], start=False, stop=True))

                    @blk.vector
                    def _(vector):
                        def emit_d(ev):
                            if ev[0] == "bcev":
                                c = ev[1]
                                vector.wait_ge(PE.h, pos[("bc", c)])
                                EV.inc(vector.tensor_copy(bcs[:, c % 2, :], p_big2[:, 0:TC]))
                            elif ev[0] == "P":
                                c = ev[1]
                                vector.wait_ge(PE.h, pos[("u1", c, 1)])
                                EV.inc(vector.tensor_copy(pb[:, c % 2, :, :]
                                                          .rearrange("p a b -> p (a b)"),
                                                          p_big[:, 0:1024]))
                            elif ev[0] == "Qm":
                                c = ev[1]
                                vector.wait_ge(PE.h, pos[("u2", c, 31, 1)])
                                EV.inc(vector.tensor_copy(qmb[:, c % 2, :, :]
                                                          .rearrange("p a b -> p (a b)"),
                                                          p_big2[:, 0:1024]))
                            elif ev[0] == "xadd":
                                c, dp = ev[1], ev[2]
                                sl = slice(TC * c, TC * (c + 1))
                                vector.wait_ge(PE.h, pos[("v2", c, 2 * dp + 1)])
                                EV.inc(vector.tensor_tensor(
                                    x[:, 2 * dp:2 * dp + 2, sl], x[:, 2 * dp:2 * dp + 2, sl],
                                    p_big2[:, 1024:2048].rearrange("p (a b) -> p a b", a=2),
                                    OP.add))
                        di = 0
                        emit_d(dplan[0])

                        def yprime(c):
                            vector.wait_ge(EV.h, EV.v)
                            sl = slice(TC * c, TC * (c + 1))
                            for cc in range(NC):
                                CAST.inc(vector.tensor_tensor(ybc[:, c % 2, cc, :], x[:, cc, sl],
                                                              bcs[:, c % 2, :], OP.mult))
                        yprime(0)
                        emit_d(dplan[1])
                        yprime(1)
                        for ev in dplan[2:]:
                            emit_d(ev)
                            if ev[0] == "bcev":
                                yprime(ev[1])

                    @blk.scalar
                    def _(scalar):
                        for c in range(NTC):
                            for fp in range(16):
                                scalar.wait_ge(PE.h, pos[("v1", c, 2 * fp + 1)])
                                if fp >= 2:
                                    scalar.wait_ge(PE.h, pos[("u2", c, 2 * fp - 3, 1)])
                                psl = p_big[:, 1024 * (fp % 2):1024 * (fp % 2) + 1024]
                                ACT.inc(scalar.activation(hb[:, 2 * (fp % 2):2 * (fp % 2) + 2, :],
                                                          psl.rearrange("p (a b) -> p a b", a=2),
                                                          AF.Gelu_apprx_tanh))

        ATT_STEPS = []
        for _kt in range(NT):
            _lo = 128 * _kt
            while _lo < 1536:
                _hi = min(1536, 512 * (_lo // 512 + 1))
                ATT_STEPS.append((_kt, _lo, _hi))
                _lo = _hi
        for _kt in range(NT):
            ATT_STEPS.append((_kt, max(1536, 128 * _kt), S))
        N2 = len(ATT_STEPS)
        IC = {}
        for _c in range(3):
            IC[_c] = next(i for i, (k, lo, hi) in enumerate(ATT_STEPS)
                          if k == 4 * _c + 3 and lo // 512 == _c)
        IC[3] = N2 - 1
        ICR = {i: c for c, i in IC.items()}

        def emit_attn_pair(p, ct, ch):
            vp, attn_fm = ct["v_pair"][p % 2], ct["attn_fm"]
            qgA, qgB, kgA, kgB = ct["qgA"], ct["qgB"], ct["kgA"], ct["kgB"]
            pt, m2t = ct["pt"], ct["m2t"]
            rdf, rdsb2, rdb, u_row = ct["rdf"], ct["rdsb2"], ct["rdb"], ct["u_row"]
            qp_r, kp_r = ct["qp1"][p % 2], ct["kp1"][p % 2]
            last = (p == NP - 1)
            prev = (p > 0)
            off_ln = 2 if prev else 0
            off_bc = 3 if prev else 0
            off_tp = 1 if prev else 0
            off_dr = 1 if prev else 0

            with nc.Block() as blk:
                pa0, pb0, xa0 = ch["PA"].v, ch["PB"].v, ch["XA"].v
                av0, ln0, bc0 = ch["AV"].v, ch["LN"].v, ch["BC"].v
                dr0, tp0 = ch["DR"].v, ch["TP"].v
                act0, ld0, row0, dmx0 = ACT.v, LD.v, ROW.v, DMX.v
                aux = (emit_attn_aux_begin(p + 1, ct, ch, dr0=dr0 + off_dr,
                                           ln0=ln0 + off_ln,
                                           act0=act0, dmx_gate=dmx0 + 96, xa0=xa0)
                       if not last else None)
                pe_sched = {}
                if aux:
                    for st, fn in aux["pe_sched"]:
                        pe_sched.setdefault(min(st, N2 - 1), []).append(fn)

                def emit_avd(tensor, i):
                    kt, lo, hi = ATT_STEPS[i]
                    w = hi - lo
                    c = lo // 512
                    stop = (kt == 4 * c + 3)
                    base = 32 * c if c < 3 else 0
                    if c == 0 and kt == 0 and prev:
                        tensor.wait_ge(ch["LN"].h, ln0 + 2)
                    if c == 3 and kt == 0:
                        if prev:
                            tensor.wait_ge(ch["DR"].h, dr0 + 1)
                        tensor.wait_ge(ch["LN"].h, ln0 + off_ln + 1)
                    sl = i % 2
                    tensor.matmul(p_big[base:base + 1,
                                        1024 + lo - 512 * c:1024 + hi - 512 * c],
                                  ones_bf[:, 0:1], pt[:, 0, sl, 0:w],
                                  start=(kt == 0), stop=stop, skip_group_check=True)
                    tensor.matmul(p_big[base:base + 1,
                                        1536 + lo - 512 * c:1536 + hi - 512 * c],
                                  ones_bf[:, 0:1], pt[:, 1, sl, 0:w],
                                  start=(kt == 0), stop=stop, skip_group_check=True)
                    tensor.matmul(p_big2[0:64, lo:hi], vp[:, kt, 0:64],
                                  pt[:, 0, sl, 0:w], start=(kt == 0), stop=stop,
                                  tile_position=(0, 0), skip_group_check=True)
                    mm = tensor.matmul(p_big2[64:128, lo:hi], vp[:, kt, 64:128],
                                       pt[:, 1, sl, 0:w], start=(kt == 0), stop=stop,
                                       tile_position=(0, 64), skip_group_check=True)
                    ch["AV"].inc(mm)

                def emit_bc(tensor, base, protect):
                    tensor.wait_ge(ch["BC"].h, base + 2)
                    mm = None
                    for h in range(2):
                        tensor.matmul(p_big[64 * h:64 * (h + 1), 0:512],
                                      r_ones[0:1, 0:64], rdsb2[0:1, 0, h, :],
                                      start=True, stop=False, skip_group_check=True)
                        mm = tensor.matmul(p_big[64 * h:64 * (h + 1), 0:512],
                                           r_ones[0:1, 0:64], rdsb2[0:1, 1, h, :],
                                           start=False, stop=True, skip_group_check=True)
                    ch["TP"].inc(mm)
                    if protect:
                        tensor.wait_ge(ch["BC"].h, base + 3)

                v_mark = (LSL[4].v + 16) if p == 0 else ct["v_marks"][p]

                @blk.tensor
                def _(tensor):
                    tensor.wait_ge(DMX.h, dmx0 + 96)
                    tensor.wait_ge(LSL[4].h, v_mark)
                    if prev:
                        emit_bc(tensor, bc0, True)
                    for i, (kt, lo, hi) in enumerate(ATT_STEPS):
                        w = hi - lo
                        first = (lo == 128 * kt)
                        ksl = slice(128 * kt, 128 * (kt + 1))
                        for fn in pe_sched.get(i, []):
                            fn(tensor)
                        if i >= 1:
                            tensor.wait_ge(ch["XA"].h, xa0 + i)
                        c_bc = ICR.get(i - 6)
                        if c_bc is not None and c_bc < 3:
                            emit_bc(tensor, bc0 + off_bc + 3 * c_bc, True)
                        mm = tensor.matmul(p_big[:, 0:w], kgA[:, ksl], qgA[:, lo:hi],
                                           start=True, stop=not first)
                        if first:
                            mm = tensor.matmul(p_big[:, 0:128], identb[:, :], m2t[:, :],
                                               start=False, stop=True)
                        ch["PA"].inc(mm)
                        mm = tensor.matmul(p_big[:, 512:512 + w], kgB[:, ksl], qgB[:, lo:hi],
                                           start=True, stop=not first)
                        if first:
                            mm = tensor.matmul(p_big[:, 512:640], identb[:, :], m2t[:, :],
                                               start=False, stop=True)
                        ch["PB"].inc(mm)
                        if i >= 1:
                            emit_avd(tensor, i - 1)
                    tensor.wait_ge(ch["XA"].h, xa0 + N2)
                    emit_avd(tensor, N2 - 1)
                    if last:
                        emit_bc(tensor, bc0 + off_bc + 9, False)

                def emit_rd(scalar, c, tail=False):
                    if not tail:
                        scalar.wait_ge(ch["AV"].h, av0 + IC[c] + 1)
                        if c >= 1:
                            scalar.wait_ge(ch["BC"].h, bc0 + off_bc + 3 * (c - 1) + 2)
                        elif prev:
                            scalar.wait_ge(ch["BC"].h, bc0 + 2)
                    base = 32 * c if c < 3 else 0
                    a1 = ch["LN"].inc(scalar.activation(
                        rdf[0:1, :, :],
                        p_big[base:base + 1, 1024:2048]
                        .rearrange("p (a b) -> p a b", a=2), AF.Ln))
                    scalar.wait_ge(ch["LN"].h, a1)
                    ch["LN"].inc(scalar.activation(rdf[0:1, :, :], rdf[0:1, :, :],
                                                   AF.Exp, scale=-1.0))

                @blk.scalar
                def _(scalar):
                    if prev:
                        emit_rd(scalar, 3, tail=True)
                    for i, (kt, lo, hi) in enumerate(ATT_STEPS):
                        w = hi - lo
                        if i >= 2:
                            scalar.wait_ge(ch["AV"].h, av0 + i - 1)
                        scalar.wait_ge(ch["PA"].h, pa0 + i + 1)
                        scalar.wait_ge(ch["PB"].h, pb0 + i + 1)
                        ch["XA"].inc(scalar.activation(
                            pt[:, :, i % 2, 0:w],
                            p_big[:, 0:1024].rearrange("p (a b) -> p a b", a=2)[:, :, 0:w],
                            AF.Exp))
                        c = ICR.get(i - 1)
                        if c is not None and c < 3:
                            emit_rd(scalar, c)
                        if aux and i == 36:
                            aux["act_tail"](scalar)
                    if last:
                        emit_rd(scalar, 3)

                @blk.vector
                def _(vector):
                    def hilo(c, tail=False):
                        if tail:
                            vector.wait_ge(ch["LN"].h, ln0 + 2)
                        else:
                            vector.wait_ge(ch["LN"].h, ln0 + off_ln + 2 * (c + 1))
                            vector.wait_ge(ch["TP"].h, tp0 + off_tp + c)
                        r = ch["BC"].inc(vector.tensor_copy(rdsb2[0:1, 0, :, :],
                                                            rdf[0:1, :, :]))
                        vector.wait_ge(ch["BC"].h, r)
                        ch["BC"].inc(vector.tensor_tensor(rdsb2[0:1, 1, :, :], rdf[0:1, :, :],
                                                          rdsb2[0:1, 0, :, :], OP.subtract))

                    def rdbcp(c, tail=False):
                        if tail:
                            vector.wait_ge(ch["TP"].h, tp0 + 1)
                        else:
                            vector.wait_ge(ch["TP"].h, tp0 + off_tp + c + 1)
                            vector.wait_ge(ch["DR"].h, dr0 + off_dr + c)
                        ch["BC"].inc(vector.tensor_copy(rdb[:, :], p_big[:, 0:512]))

                    def evac(c, tail=False):
                        if tail:
                            vector.wait_ge(ch["BC"].h, bc0 + 3)
                            ch["DR"].inc(vector.tensor_tensor(
                                attn_fm[:, p - 1, 1536:2048],
                                p_big2[:, 1536:2048], rdb[:, :], OP.mult))
                            return
                        vector.wait_ge(ch["AV"].h, av0 + IC[c] + 1)
                        vector.wait_ge(ch["BC"].h, bc0 + off_bc + 3 * c + 3)
                        ch["DR"].inc(vector.tensor_tensor(
                            attn_fm[:, p, 512 * c:512 * (c + 1)],
                            p_big2[:, 512 * c:512 * (c + 1)], rdb[:, :], OP.mult))

                    if prev:
                        hilo(3, tail=True); rdbcp(3, tail=True); evac(3, tail=True)

                    if aux:
                        aux["dve_wsum"](vector, 0)
                        aux["dve_wsum"](vector, 1)
                    hilo(0); rdbcp(0); evac(0)
                    if aux:
                        for t in range(6):
                            aux["dve_pjev"](vector, t)
                    hilo(1)
                    if aux:
                        aux["dve_pjev"](vector, 6)
                        aux["dve_pjev"](vector, 7)
                    rdbcp(1); evac(1)
                    hilo(2)
                    if aux:
                        aux["dve_red"](vector, 0, 0)
                        aux["dve_red"](vector, 1, 0)
                    rdbcp(2); evac(2)
                    if aux:
                        aux["dve_red"](vector, 0, 1)
                        aux["dve_red"](vector, 1, 1)
                    if p + 2 < NP:
                        stv = ct["wst4"][:, :, :, :].rearrange("p a c m -> p (a c) m")
                        vector.wait_ge(ch["PJ"].h, aux["pj0"] + 10)
                        vector.wait_ge(LD.h, ld0 + 16)
                        ROW.inc(vector.tensor_scalar(ct["wqb"][:, :, :], stv, 0.125,
                                                     None, OP.mult))
                        vector.wait_ge(LD.h, ld0 + 32)
                        ROW.inc(vector.tensor_copy(ct["wkb"][:, :, :], stv))
                        vector.wait_ge(LD.h, ld0 + 64)
                        ROW.inc(vector.tensor_scalar(ct["r_qkb"][0:1, 0:128],
                                                     ct["wst4"][0:1, 0, 0, 0:128],
                                                     0.125, None, OP.mult))
                        ROW.inc(vector.tensor_copy(ct["r_qkb"][0:1, 128:256],
                                                   ct["wst4"][0:1, 0, 1, 0:128]))
                    if aux:
                        aux["dve_red"](vector, 0, 2)
                        aux["dve_red"](vector, 1, 2)
                        aux["dve_red"](vector, 0, 3)
                        aux["dve_red"](vector, 1, 3)
                        aux["dve_prod"](vector)
                        aux["dve_urow"](vector)
                    if last:
                        hilo(3); rdbcp(3); evac(3)

                @blk.gpsimd
                def _(gp):
                    if aux:
                        aux["pool"](gp)

                @blk.sync
                def _(sync):
                    sync.wait_ge(DMX.h, dmx0)
                    if p == 0:
                        for s in range(4):
                            sync.wait_ge(LSL[s].h, LSL[s].v)
                        sync.wait_ge(LSL[4].h, LSL[4].v)
                        LSL[4].inc(sync.dma_start(
                            ct["v_pair"][0][:, :, :],
                            v_scr.ap().rearrange("t p m -> p t m")[:, :, 0:128]), 16)
                        ct["v_marks"][0] = LSL[4].v
                    DMX.inc(sync.dma_start(qgA[0:64, :], qp_r[0:64, :]), 16)
                    DMX.inc(sync.dma_start(qgB[0:64, :], qp_r[64:128, :]), 16)
                    DMX.inc(sync.dma_start(kgA[0:64, :], kp_r[0:64, :]), 16)
                    DMX.inc(sync.dma_start(kgB[0:64, :], kp_r[64:128, :]), 16)
                    DMX.inc(sync.dma_start(kgA[64:65, :], u_row[0:1, 0, :]), 16)
                    DMX.inc(sync.dma_start(kgB[64:65, :], u_row[0:1, 1, :]), 16)
                    if p + 1 < NP:
                        sync.wait_ge(LSL[4].h, LSL[4].v)
                        LSL[4].inc(sync.dma_start(
                            ct["v_pair"][(p + 1) % 2][:, :, :],
                            v_scr.ap().rearrange("t p m -> p t m")
                            [:, :, 128 * (p + 1):128 * (p + 2)]), 16)
                        ct["v_marks"][p + 1] = LSL[4].v
                    if p + 2 < NP:
                        pp = p + 2
                        stv = ct["wst4"][:, :, :, :].rearrange("p a c m -> p (a c) m")
                        srcq = din["qw"].ap().rearrange("(c p) m -> p c m", p=128) \
                            [:, :, 128 * pp:128 * (pp + 1)]
                        LD.inc(sync.dma_start(stv, srcq), 16)
                        sync.wait_ge(ROW.h, row0 + 1)
                        srck = din["kw"].ap().rearrange("(c p) m -> p c m", p=128) \
                            [:, :, 128 * pp:128 * (pp + 1)]
                        LD.inc(sync.dma_start(stv, srck), 16)
                        sync.wait_ge(ROW.h, row0 + 2)
                        LD.inc(sync.dma_start(ct["wst4"][0:1, 0, 0, 0:128],
                                              din["qb"].ap()[0:1, 128 * pp:128 * (pp + 1)]), 16)
                        LD.inc(sync.dma_start(ct["wst4"][0:1, 0, 1, 0:128],
                                              din["kb"].ap()[0:1, 128 * pp:128 * (pp + 1)]), 16)
                        sync.wait_ge(LD.h, LD.v)

        def emit_qk_weight_loadcast(pp, ct):
            stv = ct["wst4"][:, :, :, :].rearrange("p a c m -> p (a c) m")
            with nc.Block() as blk:
                ld0, row0 = LD.v, ROW.v

                @blk.sync
                def _(sync):
                    srcq = din["qw"].ap().rearrange("(c p) m -> p c m", p=128) \
                        [:, :, 128 * pp:128 * (pp + 1)]
                    LD.inc(sync.dma_start(stv, srcq), 16)
                    sync.wait_ge(ROW.h, row0 + 1)
                    srck = din["kw"].ap().rearrange("(c p) m -> p c m", p=128) \
                        [:, :, 128 * pp:128 * (pp + 1)]
                    LD.inc(sync.dma_start(stv, srck), 16)
                    sync.wait_ge(ROW.h, row0 + 2)
                    LD.inc(sync.dma_start(ct["wst4"][0:1, 0, 0, 0:128],
                                          din["qb"].ap()[0:1, 128 * pp:128 * (pp + 1)]), 16)
                    LD.inc(sync.dma_start(ct["wst4"][0:1, 0, 1, 0:128],
                                          din["kb"].ap()[0:1, 128 * pp:128 * (pp + 1)]), 16)
                    sync.wait_ge(LD.h, LD.v)

                @blk.vector
                def _(vector):
                    vector.wait_ge(LD.h, ld0 + 16)
                    ROW.inc(vector.tensor_scalar(ct["wqb"][:, :, :], stv, 0.125, None, OP.mult))
                    vector.wait_ge(LD.h, ld0 + 32)
                    ROW.inc(vector.tensor_copy(ct["wkb"][:, :, :], stv))
                    vector.wait_ge(LD.h, ld0 + 64)
                    ROW.inc(vector.tensor_scalar(ct["r_qkb"][0:1, 0:128],
                                                 ct["wst4"][0:1, 0, 0, 0:128],
                                                 0.125, None, OP.mult))
                    ROW.inc(vector.tensor_copy(ct["r_qkb"][0:1, 128:256],
                                               ct["wst4"][0:1, 0, 1, 0:128]))

        def emit_attn_aux_begin(pp, ct, ch, dr0=None, ln0=None, act0=None,
                                dmx_gate=None, xa0=None, pregions=False):
            yb2, r_mr = ct["yb2"], ct["r_mr"]
            wqb, wkb = ct["wqb"], ct["wkb"]
            r_qkb, r_wsp, nsq = ct["r_qkb"], ct["r_wsp"], ct["nsq"]
            u_sc, u_row, ones_sr = ct["u_sc"], ct["u_row"], ct["ones_sr"]
            qp_w, kp_w = ct["qp1"][pp % 2], ct["kp1"][pp % 2]
            in_pair = dr0 is not None
            ws0, pj0, nq0, uc0 = ch["WS"].v, ch["PJ"].v, ch["NQ"].v, ch["UC"].v

            red_order = [(0, 0), (1, 0), (0, 1), (1, 1), (0, 2), (1, 2), (0, 3), (1, 3)]
            uc_after = {}
            acc = 0
            for (wh, j) in red_order:
                acc += 4
                uc_after[(wh, j)] = uc0 + acc
            uc_prod = uc0 + acc + 1

            def mk_ws(which, st):
                def f(tensor):
                    if xa0 is not None:
                        tensor.wait_ge(ch["XA"].h, xa0 + st)
                    wb_ = wqb if which == 0 else wkb
                    mm = None
                    for c in range(NC):
                        mm = tensor.matmul(p_big[0:1, 0:128], neg_bf[:, :],
                                           wb_[:, c, :], start=(c == 0), stop=(c == NC - 1),
                                           skip_group_check=True)
                    ch["PJ"].inc(mm)
                    tensor.wait_ge(ch["WS"].h, ws0 + which + 1)
                return f

            def mk_pj(which, tci):
                t = 4 * which + tci
                def f(tensor):
                    wb_ = wqb if which == 0 else wkb
                    wsl = slice(128 * which, 128 * (which + 1))
                    sl = slice(TC * tci, TC * (tci + 1))
                    psl = p_big[:, 1024:1536] if pregions else p_big2[:, 0:512]
                    tensor.wait_ge(ch["WS"].h, ws0 + 2 + t)
                    if in_pair and t == 0:
                        tensor.wait_ge(ch["DR"].h, dr0 + 1)
                    for c in range(NC):
                        tensor.matmul(psl, wb_[:, c, :], yb2[:, c, sl],
                                      start=(c == 0), stop=False)
                    tensor.matmul(psl, r_wsp[0:1, wsl], r_mr[0:1, sl], start=False, stop=False)
                    ch["PJ"].inc(tensor.matmul(psl, r_qkb[0:1, wsl], r_ones[0:1, 0:TC],
                                               start=False, stop=True))
                return f

            def nrm_region(which, h, j):
                lane = 2 * which + h
                if pregions:
                    if lane < 3:
                        return p_big, 32 * lane, (0 if j % 2 == 0 else 512)
                    return (p_big, 0, 1024) if j % 2 == 0 else (p_big, 0, 1536)
                if lane < 3:
                    return p_big2, 32 * lane, (512 if j % 2 == 0 else 0)
                return (p_big, 64, 1024) if j % 2 == 0 else (p_big2, 0, 1024)

            def mk_nrm(which, h, j):
                def f(tensor):
                    t = 4 * which + j
                    tensor.wait_ge(ch["NQ"].h, nq0 + t + 1)
                    if in_pair and (which, h, j) == (0, 0, 0):
                        tensor.wait_ge(ch["DR"].h, dr0 + 2)
                    if in_pair and (which, h) == (1, 1) and j == 0:
                        tensor.wait_ge(ch["LN"].h, ln0 + 6)
                    if in_pair and (which, h) == (1, 1) and j == 1:
                        tensor.wait_ge(ch["DR"].h, dr0 + 3)
                    if pregions and (which, h) == (1, 1) and j <= 1:
                        tensor.wait_ge(ch["WS"].h, ws0 + 10)
                    if j == 1 and 2 * which + h < 3:
                        tensor.wait_ge(ch["WS"].h, ws0 + 10)
                    if j >= 2:
                        tensor.wait_ge(ch["UC"].h, uc_after[(which, j - 2)])
                    pb, base, coff = nrm_region(which, h, j)
                    ch["PJ"].inc(tensor.matmul(pb[base:base + 1, coff:coff + 512],
                                               ct["ones2b"][:, h:h + 1], nsq[:, t % 4, :],
                                               start=True, stop=True, skip_group_check=True))
                return f

            items = [(6, ("ws", 0)), (8, ("ws", 1))]
            for t in range(8):
                items.append((16 + t, ("pj", t // 4, t % 4)))
            for j in range(4):
                for h in range(2):
                    items.append((25 + 2 * j, ("nrm", 0, h, j)))
                items.append((26 + 2 * j, ("nrm", 1, 0, j)))
                items.append((max(28 if j == 0 else 30, 26 + 2 * j), ("nrm", 1, 1, j)))
            items.sort(key=lambda it: it[0])
            nrm_count = {}
            for pos, (st, tag) in enumerate(items):
                if tag[0] == "nrm":
                    nrm_count[tag[1:]] = pj0 + pos + 1

            pe_sched = []
            for st, tag in items:
                if tag[0] == "ws":
                    pe_sched.append((st, mk_ws(tag[1], st)))
                elif tag[0] == "pj":
                    pe_sched.append((st, mk_pj(tag[1], tag[2])))
                else:
                    pe_sched.append((st, mk_nrm(tag[1], tag[2], tag[3])))

            def dve_wsum(vector, which):
                vector.wait_ge(ch["PJ"].h, pj0 + which + 1)
                ch["WS"].inc(vector.tensor_copy(r_wsp[0:1, 128 * which:128 * (which + 1)],
                                                p_big[0:1, 0:128]))

            def dve_pjev(vector, t):
                which, tci = t // 4, t % 4
                dstp = qp_w if which == 0 else kp_w
                vector.wait_ge(ch["PJ"].h, pj0 + 2 + t + 1)
                psl = p_big[:, 1024:1536] if pregions else p_big2[:, 0:512]
                ch["WS"].inc(vector.tensor_copy(dstp[:, TC * tci:TC * (tci + 1)], psl))

            def dve_red(vector, which, j):
                for h in range(2):
                    pb, base, coff = nrm_region(which, h, j)
                    l = 2 * which + h
                    vector.wait_ge(ch["PJ"].h, nrm_count[(which, h, j)])
                    vector.wait_ge(ch["UC"].h, ch["UC"].v)
                    if j == 0:
                        r = ch["UC"].inc(vector.tensor_reduce(
                            u_sc[0:1, l:l + 1], pb[base:base + 1, coff:coff + 512],
                            AX.X, OP.max))
                        vector.wait_ge(ch["UC"].h, r)
                        ch["UC"].inc(vector.tensor_copy(u_sc[0:1, 8:9], u_sc[0:1, l:l + 1]))
                    else:
                        r = ch["UC"].inc(vector.tensor_reduce(
                            u_sc[0:1, 8:9], pb[base:base + 1, coff:coff + 512],
                            AX.X, OP.max))
                        vector.wait_ge(ch["UC"].h, r)
                        ch["UC"].inc(vector.tensor_tensor(u_sc[0:1, l:l + 1],
                                                          u_sc[0:1, l:l + 1],
                                                          u_sc[0:1, 8:9], OP.max))

            def dve_prod(vector):
                vector.wait_ge(ch["UC"].h, ch["UC"].v)
                ch["UC"].inc(vector.tensor_tensor(u_sc[0:1, 4:6], u_sc[0:1, 0:2],
                                                  u_sc[0:1, 2:4], OP.mult))

            def act_tail(scalar):
                scalar.wait_ge(ch["UC"].h, uc_prod)
                a1 = ACT.inc(scalar.activation(u_sc[0:1, 6:8], u_sc[0:1, 4:6], AF.Ln))
                scalar.wait_ge(ACT.h, a1)
                ACT.inc(scalar.activation(u_sc[0:1, 6:8], u_sc[0:1, 6:8], AF.Exp, scale=0.5))

            def dve_urow(vector):
                vector.wait_ge(ACT.h, (act0 if act0 is not None else 0) + 2)
                if dmx_gate is not None:
                    vector.wait_ge(DMX.h, dmx_gate)
                for h in range(2):
                    for q2 in range(2):
                        ch["WS"].inc(vector.tensor_scalar(
                            u_row[0:1, h, 1024 * q2:1024 * (q2 + 1)], ones_sr[0:1, :],
                            u_sc[0:1, 6 + h:7 + h], -1.0, OP.mult, OP.mult))

            def pool(gp):
                for which in range(2):
                    srcp = qp_w if which == 0 else kp_w
                    for j in range(4):
                        t = 4 * which + j
                        gp.wait_ge(ch["WS"].h, ws0 + 2 + t + 1)
                        if t >= 4:
                            jp, whp = (t - 4) % 4, (t - 4) // 4
                            gp.wait_ge(ch["PJ"].h, max(nrm_count[(whp, 0, jp)],
                                                       nrm_count[(whp, 1, jp)]))
                        ch["NQ"].inc(gp.tensor_tensor(nsq[:, t % 4, :],
                                                      srcp[:, 512 * j:512 * (j + 1)],
                                                      srcp[:, 512 * j:512 * (j + 1)], OP.mult))

            return dict(pe_sched=pe_sched, dve_wsum=dve_wsum, dve_pjev=dve_pjev,
                        dve_red=dve_red, dve_prod=dve_prod, act_tail=act_tail,
                        dve_urow=dve_urow, pool=pool, pj0=pj0)

        def emit_attn_sublayer():
            with contextlib.ExitStack() as actx:
                attn_fm = mk(actx, "attn_fm", [128, NP, S], BF16)
                bctx = actx.enter_context(contextlib.ExitStack())
                yb2 = mk(bctx, "yb2", [128, NC, S], BF16)
                r_mr = mk(bctx, "r_mr2", [1, S], BF16)

                ct = {}
                ct["yb2"], ct["r_mr"] = yb2, r_mr
                ct["v_marks"] = {}
                r_scv = mk(bctx, "r_scv", [1, D], BF16)
                ct["attn_fm"] = attn_fm
                ct["qp1"] = [mk(bctx, "qp1", [128, S], BF16) for _ in range(2)]
                ct["kp1"] = [mk(bctx, "kp1", [128, S], BF16) for _ in range(2)]
                ct["m2t"] = mk(bctx, "m2t", [128, 128], BF16)
                ct["ones2b"] = mk(bctx, "ones2b", [128, 2], BF16)
                ct["u_sc"] = mk(bctx, "u_sc", [1, 16])
                ct["u_row"] = mk(bctx, "u_row", [1, 2, S], BF16)
                ct["ones_sr"] = mk(bctx, "ones_sr", [1, 1024], BF16)
                ct["wst4"] = mk(bctx, "wst4", [128, 2, 4, 128])
                ct["wqb"] = mk(bctx, "wqb", [128, NC, 128], BF16)
                ct["wkb"] = mk(bctx, "wkb", [128, NC, 128], BF16)
                ct["r_qkb"] = mk(bctx, "r_qkb", [1, 256], BF16)
                ct["r_wsp"] = mk(bctx, "r_wsp", [1, 256], BF16)
                ct["nsq"] = mk(bctx, "nsq", [128, 4, 512], BF16)

                ch = {}
                for name in ["PA", "PB", "XA", "PJ", "WS",
                             "NQ", "UC", "AV", "TP", "DR", "BC", "LN"]:
                    h = bctx.enter_context(nc.semaphore("c_" + name))
                    ch[name] = _H([h, 0])

                with nc.Block() as blk:
                    @blk.sync
                    def _(sync):
                        LD.inc(sync.dma_start(ct["m2t"][:, :], din["masksb"].ap()[1]), 16)
                        sync.wait_ge(LD.h, LD.v)

                    @blk.gpsimd
                    def _(gp):
                        g1 = ROW.inc(gp.memset(ct["ones2b"][:, :], 0.0))
                        gp.wait_ge(ROW.h, g1)
                        g2 = ROW.inc(gp.memset(ct["ones2b"][0:64, 0:1], 1.0))
                        gp.wait_ge(ROW.h, g2)
                        gp.memset(ct["ones2b"][64:128, 1:2], 1.0)
                        gp.memset(ct["ones_sr"][0:1, :], 1.0)

                vctx0 = bctx.enter_context(contextlib.ExitStack())
                wb = mk(vctx0, "wb", [128, NC, D], BF16)
                wstage0 = mk(vctx0, "wstage0", [128, 2048])
                with contextlib.ExitStack() as rctx:
                    r_rstd = mk(rctx, "r_rstd2", [1, S])
                    srcq0 = din["qw"].ap().rearrange("(c p) m -> p c m", p=128)[:, :, 0:128]
                    srck0 = din["kw"].ap().rearrange("(c p) m -> p c m", p=128)[:, :, 0:128]
                    wl2 = ([(din["vw"].ap().rearrange("(c p) m -> p c m", p=128)
                             [:, c:c + 1, 512 * h:512 * (h + 1)],
                             wb[:, c:c + 1, 512 * h:512 * (h + 1)], None)
                            for c in range(NC) for h in range(2)]
                           + [(srcq0[:, 2 * i:2 * i + 2, :],
                               ct["wqb"][:, 2 * i:2 * i + 2, :], 0.125) for i in range(4)]
                           + [(srck0[:, 2 * i:2 * i + 2, :],
                               ct["wkb"][:, 2 * i:2 * i + 2, :], None) for i in range(4)]
                           + [(din["qb"].ap()[0:1, 0:128], ct["r_qkb"][0:1, 0:128], 0.125),
                              (din["kb"].ap()[0:1, 0:128], ct["r_qkb"][0:1, 128:256], None),
                              (din["vb"].ap()[0:1, 0:512], r_scv[0:1, 0:512], None),
                              (din["vb"].ap()[0:1, 512:1024], r_scv[0:1, 512:1024], None)])
                    emit_ln_stats(r_rstd, r_mr, extra_loads=wl2, stage=wstage0)
                    with nc.Block() as blk:
                        pe_y0, ca_y0 = PE.v, CAST.v

                        @blk.tensor
                        def _(tensor):
                            for tci in range(NTC):
                                if tci >= 2:
                                    tensor.wait_ge(CAST.h, ca_y0 + 8 * (tci - 1))
                                sl = slice(TC * tci, TC * (tci + 1))
                                PE.inc(tensor.matmul(p_big2[:, 512 * (tci % 2):512 * (tci % 2) + TC],
                                                     ones_f32[0:1, :], r_rstd[0:1, sl],
                                                     start=True, stop=True))

                        @blk.vector
                        def _(vector):
                            for tci in range(NTC):
                                sl = slice(TC * tci, TC * (tci + 1))
                                vector.wait_ge(PE.h, pe_y0 + tci + 1)
                                for c in range(NC):
                                    CAST.inc(vector.tensor_tensor(
                                        yb2[:, c, sl], x[:, c, sl],
                                        p_big2[:, 512 * (tci % 2):512 * (tci % 2) + TC], OP.mult))



                with contextlib.ExitStack() as vctx:
                    vaug_l = mk(vctx, "vaug_l", [2, S], BF16)
                    vaug_r = mk(vctx, "vaug_r", [2, D], BF16)

                    with nc.Block() as blk:
                        row_v0 = ROW.v

                        @blk.gpsimd
                        def _(gp):
                            ROW.inc(gp.memset(vaug_l[0:2, :], 1.0))

                        @blk.vector
                        def _(vector):
                            vector.wait_ge(ROW.h, row_v0 + 1)
                            vector.tensor_copy(vaug_l[0:1, :], r_mr[0:1, :])

                        @blk.sync
                        def _(sync):
                            LD.inc(sync.dma_start(vaug_r[1:2, :], r_scv[0:1, 0:D]), 16)
                            sync.wait_ge(LD.h, LD.v)

                    vb_s = mk(vctx, "vb_s", [128, 4, 512], BF16)
                    with nc.Block() as blk:
                        pe0, ev0, row_lc, ld_vb = PE.v, EV.v, ROW.v, LD.v
                        lsl0 = [LSL[s].v for s in range(4)]
                        aux0 = emit_attn_aux_begin(0, ct, ch, act0=ACT.v, pregions=True)
                        ev_vw = EV.v + 1

                        def vwsum_pe(tensor):
                            mm = None
                            for c in range(NC):
                                for jj in (0, 512):
                                    mm = tensor.matmul(p_big[0:1, jj:jj + 512], neg_bf[:, :],
                                                       wb[:, c, jj:jj + 512],
                                                       start=(c == 0), stop=(c == NC - 1),
                                                       skip_group_check=True)
                            PE.inc(mm)
                            tensor.wait_ge(EV.h, ev_vw)

                        g_sched = {0: [vwsum_pe]}
                        for st, fn in aux0["pe_sched"]:
                            if 16 <= st <= 23:
                                gk = 10 + (st - 16)
                            else:
                                gk = min(31, max(2, int((st - 3) * 0.7) + 2))
                            g_sched.setdefault(gk, []).append(fn)

                        @blk.tensor
                        def _(tensor):
                            for t in range(NT):
                                tsl = slice(128 * t, 128 * (t + 1))
                                for half in range(2):
                                    g = (2 * t + half)
                                    for fn in g_sched.get(g, []):
                                        fn(tensor)
                                    if g >= 4:
                                        tensor.wait_ge(EV.h, ev0 + g - 2)
                                    psl = p_big2[:, 512 * (g % 4):512 * (g % 4) + 512]
                                    nsl = slice(512 * half, 512 * (half + 1))
                                    for c in range(NC):
                                        tensor.matmul(psl, yb2[:, c, tsl], wb[:, c, nsl],
                                                      start=(c == 0), stop=False)
                                    PE.inc(tensor.matmul(psl, vaug_l[:, tsl], vaug_r[:, nsl],
                                                         start=False, stop=True))

                        @blk.vector
                        def _(vector):
                            dvepl = []
                            for t in range(NT):
                                for half in range(2):
                                    g = (2 * t + half)

                                    def vev(vector, g=g):
                                        vector.wait_ge(PE.h, pe0 + g + 2)
                                        if g >= 4:
                                            vector.wait_ge(LSL[g % 4].h,
                                                           lsl0[g % 4] + 16 * (g // 4))
                                        EV.inc(vector.tensor_copy(
                                            vb_s[:, g % 4, :],
                                            p_big2[:, 512 * (g % 4):512 * (g % 4) + 512]))
                                    dvepl.append((g + 0.5, vev))
                            def vwsum_ev(v):
                                v.wait_ge(PE.h, pe0 + 1)
                                v.wait_ge(LD.h, ld_vb)
                                EV.inc(v.tensor_copy(vaug_r[0:1, 0:D], p_big[0:1, 0:1024]))
                            dvepl.append((0.2, vwsum_ev))
                            dvepl.append((4.2, lambda v: aux0["dve_wsum"](v, 0)))
                            dvepl.append((5.2, lambda v: aux0["dve_wsum"](v, 1)))
                            for t in range(8):
                                dvepl.append((10.6 + t,
                                              lambda v, t=t: aux0["dve_pjev"](v, t)))
                            ro = [(0, 0), (1, 0), (0, 1), (1, 1), (0, 2), (1, 2), (0, 3), (1, 3)]
                            for i, (wh, j) in enumerate(ro):
                                dvepl.append((19.0 + i, lambda v, wh=wh, j=j:
                                              aux0["dve_red"](v, wh, j)))
                            dvepl.append((27.0, lambda v: aux0["dve_prod"](v)))
                            dvepl.append((27.5, lambda v: aux0["dve_urow"](v)))
                            ld_lc = LD.v
                            stvd = ct["wst4"][:, :, :, :].rearrange("p a c m -> p (a c) m")

                            def lc_cast_q(v):
                                v.wait_ge(ch["PJ"].h, aux0["pj0"] + 10)
                                v.wait_ge(LD.h, ld_lc + 16)
                                ROW.inc(v.tensor_scalar(ct["wqb"][:, :, :], stvd, 0.125,
                                                        None, OP.mult))

                            def lc_cast_k(v):
                                v.wait_ge(LD.h, ld_lc + 32)
                                ROW.inc(v.tensor_copy(ct["wkb"][:, :, :], stvd))

                            def lc_cast_b(v):
                                v.wait_ge(LD.h, ld_lc + 64)
                                ROW.inc(v.tensor_scalar(ct["r_qkb"][0:1, 0:128],
                                                        ct["wst4"][0:1, 0, 0, 0:128],
                                                        0.125, None, OP.mult))
                                ROW.inc(v.tensor_copy(ct["r_qkb"][0:1, 128:256],
                                                      ct["wst4"][0:1, 0, 1, 0:128]))
                            dvepl.append((18.2, lc_cast_q))
                            dvepl.append((20.7, lc_cast_k))
                            dvepl.append((26.7, lc_cast_b))
                            for _, fn in sorted(dvepl, key=lambda x: x[0]):
                                fn(vector)

                        @blk.scalar
                        def _(scalar):
                            aux0["act_tail"](scalar)

                        @blk.gpsimd
                        def _(gp):
                            aux0["pool"](gp)

                        @blk.sync
                        def _(sync):
                            stv = ct["wst4"][:, :, :, :].rearrange("p a c m -> p (a c) m")
                            LD.inc(sync.dma_start(
                                stv, din["qw"].ap().rearrange("(c p) m -> p c m", p=128)
                                [:, :, 128:256]), 16)
                            for t in range(NT):
                                for half in range(2):
                                    g = (2 * t + half)
                                    sync.wait_ge(EV.h, ev0 + g + 2)
                                    LSL[g % 4].inc(sync.dma_start(
                                        v_scr.ap()[t, :, 512 * half:512 * (half + 1)],
                                        vb_s[:, g % 4, :]), 16)
                                    if g == 20:
                                        sync.wait_ge(ROW.h, row_lc + 1)
                                        LD.inc(sync.dma_start(
                                            stv, din["kw"].ap()
                                            .rearrange("(c p) m -> p c m", p=128)
                                            [:, :, 128:256]), 16)
                                    if g == 26:
                                        sync.wait_ge(ROW.h, row_lc + 2)
                                        LD.inc(sync.dma_start(
                                            ct["wst4"][0:1, 0, 0, 0:128],
                                            din["qb"].ap()[0:1, 128:256]), 16)
                                        LD.inc(sync.dma_start(
                                            ct["wst4"][0:1, 0, 1, 0:128],
                                            din["kb"].ap()[0:1, 128:256]), 16)
                            sync.wait_ge(LD.h, LD.v)
                            for s in range(4):
                                sync.wait_ge(LSL[s].h, LSL[s].v)

                vctx0.close()

                ct["qgA"] = mk(bctx, "qgA", [65, S], BF16)
                ct["qgB"] = mk(bctx, "qgB", [65, S], BF16)
                ct["kgA"] = mk(bctx, "kgA", [65, S], BF16)
                ct["kgB"] = mk(bctx, "kgB", [65, S], BF16)
                ct["pt"] = mk(bctx, "pt", [128, 2, 2, 512], BF16)
                ct["rdf"] = mk(bctx, "rdf", [1, 2, 512])
                ct["rdsb2"] = mk(bctx, "rdsb2", [1, 2, 2, 512], BF16)
                ct["rdb"] = mk(bctx, "rdb", [128, 512])
                ct["v_pair"] = [mk(bctx, "v_pair", [128, NT, 128], BF16) for _ in range(2)]

                with nc.Block() as blk:
                    @blk.gpsimd
                    def _(gp):
                        gp.memset(ct["qgA"][64:65, :], 1.0)
                        gp.memset(ct["qgB"][64:65, :], 1.0)

                for p in range(NP):
                    emit_attn_pair(p, ct, ch)

                bctx.close()
                with contextlib.ExitStack() as octx:
                    wb2m = mk(octx, "wb2m", [128, 2, NC, 128], BF16)
                    wst3 = mk(octx, "wst3", [128, 2, NC, 128])
                    r_ob = mk(octx, "r_ob", [1, D], BF16)
                    obst = mk(octx, "obst", [1, D])
                    osrc = din["ow"].ap().rearrange("(c p) m -> p c m", p=128)

                    with nc.Block() as blk:
                        pe0, ev0, row_o0 = PE.v, EV.v, ROW.v
                        xl0 = [LSL[8].v, LSL[9].v]

                        @blk.sync
                        def _(sync):
                            LSL[8].inc(sync.dma_start(
                                wst3[:, 0, :, :], osrc[:, :, 0:128]), 16)
                            LD.inc(sync.dma_start(obst[0:1, :], din["ob"].ap()), 16)
                            for m in range(1, NC):
                                sync.wait_ge(ROW.h, row_o0 + m + 1)
                                LSL[8 + m % 2].inc(sync.dma_start(
                                    wst3[:, m % 2, :, :],
                                    osrc[:, :, 128 * m:128 * (m + 1)]), 16)

                        @blk.gpsimd
                        def _(gp):
                            gp.wait_ge(LD.h, LD.v)
                            ROW.inc(gp.tensor_copy(r_ob[0:1, :], obst[0:1, :]))
                            for m in range(NC):
                                gp.wait_ge(LSL[8 + m % 2].h,
                                           xl0[m % 2] + 16 * (m // 2 + 1))
                                if m >= 2:
                                    gp.wait_ge(PE.h, pe0 + 4 * (m - 1))
                                ROW.inc(gp.tensor_copy(wb2m[:, m % 2, :, :],
                                                       wst3[:, m % 2, :, :]))

                        @blk.tensor
                        def _(tensor):
                            tensor.wait_ge(ROW.h, row_o0 + 1)
                            i = 0
                            for m in range(NC):
                                tensor.wait_ge(ROW.h, row_o0 + m + 2)
                                for tci in range(NTC):
                                    if i >= 2:
                                        tensor.wait_ge(EV.h, ev0 + i - 1)
                                    sl = slice(TC * tci, TC * (tci + 1))
                                    psl = p_big[:, 512 * (i % 2):512 * (i % 2) + TC]
                                    for hc in range(NC):
                                        tensor.matmul(psl, wb2m[:, m % 2, hc, :],
                                                      ct["attn_fm"][:, hc, sl], start=(hc == 0), stop=False)
                                    PE.inc(tensor.matmul(psl, r_ob[0:1, 128 * m:128 * (m + 1)],
                                                         r_ones[0:1, 0:TC], start=False, stop=True))
                                    i += 1

                        @blk.vector
                        def _(vector, pe0=pe0):
                            i = 0
                            for m in range(NC):
                                for tci in range(NTC):
                                    vector.wait_ge(PE.h, pe0 + i + 1)
                                    sl = slice(TC * tci, TC * (tci + 1))
                                    EV.inc(vector.tensor_tensor(x[:, m, sl], x[:, m, sl],
                                                                p_big[:, 512 * (i % 2):512 * (i % 2) + TC],
                                                                OP.add))
                                    i += 1

        def emit_dump_fm(dram, src3):
            with nc.Block() as blk:
                @blk.sync
                def _(sync):
                    for c in range(NC):
                        ST.inc(sync.dma_start(dram.ap()[128 * c:128 * (c + 1), :], src3[:, c, :]), 16)
                    sync.wait_ge(ST.h, ST.v)

        emit_ffn_sublayer("f1")
        if "x1" in dbg:
            emit_dump_fm(dbg["x1"], x)
        if PHASES >= 2:
            emit_attn_sublayer()
            if "x2" in dbg:
                emit_dump_fm(dbg["x2"], x)
        if PHASES >= 3:
            emit_ffn_sublayer("f2")

        with nc.Block() as blk:
            @blk.scalar
            def _(scalar):
                scalar.wait_ge(ST.h, ST.v)

    return nc


def host_inputs(inputs):
    x = np.asarray(inputs["L0_attn_x"])
    Qw = np.asarray(inputs["L0_attn_Q_w"]); Qb = np.asarray(inputs["L0_attn_Q_b"])
    Kw = np.asarray(inputs["L0_attn_K_w"]); Kb = np.asarray(inputs["L0_attn_K_b"])
    Vw = np.asarray(inputs["L0_attn_V_w"]); Vb = np.asarray(inputs["L0_attn_V_b"])
    Ow = np.asarray(inputs["L0_attn_O_w"]); Ob = np.asarray(inputs["L0_attn_O_b"])
    masks = np.stack([np.triu(np.full((128, 128), NEG, np.float32), 1),
                      np.tril(np.full((128, 128), NEG, np.float32), -1)])
    common = dict(
        qw=np.ascontiguousarray(Qw.transpose(1, 0, 2).reshape(D, D)),
        kw=np.ascontiguousarray(Kw.transpose(1, 0, 2).reshape(D, D)),
        vw=np.ascontiguousarray(Vw.transpose(1, 0, 2).reshape(D, D)),
        ow=np.ascontiguousarray(Ow.reshape(D, D)),
        qb=Qb.reshape(1, D), kb=Kb.reshape(1, D), vb=Vb.reshape(1, D),
        ob=Ob.reshape(1, D),
        u1=np.asarray(inputs["L0_ffn_U1"]), v1=np.asarray(inputs["L0_ffn_V1"]),
        b1=np.asarray(inputs["L0_ffn_b1"]).reshape(1, DF),
        u2=np.asarray(inputs["L0_ffn_U2"]), v2=np.asarray(inputs["L0_ffn_V2"]),
        b2=np.asarray(inputs["L0_ffn_b2"]).reshape(1, D),
        ident=np.eye(128, dtype=np.float32),
        masks=masks,
        masksb=masks.astype(ml_dtypes.bfloat16),
    )
    in_maps = []
    for b in range(8):
        m = dict(common)
        m["xT"] = np.ascontiguousarray(x[b].T)
        in_maps.append(m)
    return in_maps


def kernel(**inputs):
    in_maps = host_inputs(inputs)
    nc = build_nc()
    res = run_bass_kernel_spmd(nc, in_maps, core_ids=list(range(8)))
    out = np.empty((8, S, D), np.float32)
    for b in range(8):
        out[b] = res.results[b]["outT"].T
    return out
